# version 06
# baseline (speedup 1.0000x reference)
import numpy as np
import ml_dtypes
import concourse.bass as bass
import concourse.mybir as mybir
from concourse.bass_utils import run_bass_kernel_spmd

F32 = mybir.dt.float32
BF16 = mybir.dt.bfloat16
AF = mybir.ActivationFunctionType
AL = mybir.AluOpType
AX = mybir.AxisListType.X

D = 4096
S = 4096
T = 2048
H = 32
NCH = 32
FF = 8192
ISQ = float(1.0 / np.sqrt(128.0))
EPS_LN = 1e-5
EPS_RMS = 1e-6
R4096 = float(1.0 / 4096.0)

ENGS = ("tensor", "vector", "scalar", "sync", "gpsimd")


class Plan:
    def __init__(self, nc):
        self.nc = nc
        self.ops = {e: [] for e in ENGS}
        self.sems = {}
        self.cnt = {}
        self.lastwait = {e: {} for e in ENGS}

    def sem(self, name):
        self.sems[name] = self.nc.alloc_semaphore(name)
        self.cnt[name] = 0

    def op(self, eng, fn, waits=(), inc=None, inc_n=1):
        m = {}
        for w in waits:
            if w is None:
                continue
            s, v = w
            m[s] = max(v, m.get(s, 0))
        lw = self.lastwait[eng]
        ws = []
        for s, v in m.items():
            if lw.get(s, 0) >= v:
                continue
            ws.append((s, v))
            lw[s] = v
        tok = None
        if inc:
            self.cnt[inc] += inc_n
            tok = (inc, self.cnt[inc])
        self.ops[eng].append((ws, fn, (inc, inc_n) if inc else None))
        return tok

    def dma(self, eng, out, in_, waits=(), sem=None, slow=False):
        sem = sem or ("st" if eng == "sync" else "ld")

        def fn(E, out=out, in_=in_, slow=slow):
            return E.dma_start(out=out, in_=in_, allow_slow_non_contiguous=slow)
        return self.op(eng, fn, waits, inc=sem, inc_n=16)

    def tok(self, sem):
        return (sem, self.cnt[sem])

    def barrier(self):
        snap = [(s, v) for s, v in self.cnt.items() if v > 0]
        for e in ENGS:
            self.op(e, None, waits=snap)

    def replay(self, E, eng):
        for ws, fn, inc in self.ops[eng]:
            for s, v in ws:
                E.wait_ge(self.sems[s], v)
            if fn is None:
                continue
            inst = fn(E)
            if inc:
                inst.then_inc(self.sems[inc[0]], inc[1])


class Slot:
    def __init__(self):
        self.ready = None
        self.consumers = []

    def produce(self, tok):
        self.ready = tok
        self.consumers = []

    def war_waits(self):
        w = list(self.consumers)
        if self.ready is not None:
            w.append(self.ready)
        return w

    def use(self, tok):
        if tok is not None:
            self.consumers.append(tok)


def MM(out, lhsT, rhs, start, stop):
    return lambda E: E.matmul(out, lhsT, rhs, start=start, stop=stop)


def TR(out, in_, ident):
    return lambda E: E.transpose(out, in_, ident)


def build_nc(dbg=False, paranoid=0):
    nc = bass.Bass("TRN2", target_bir_lowering=False)
    pl = Plan(nc)
    for s in ("ld", "st", "t", "v", "a", "sf"):
        pl.sem(s)

    x_in = nc.dram_tensor("x", [S, D], F32, kind="ExternalInput").ap()
    W = {}
    for nm, shp in [("Wq", [D, D]), ("Wk", [D, D]), ("Wv", [D, D]),
                    ("Wo", [D, D]), ("w12", [D, FF]), ("w3", [D, D])]:
        W[nm] = nc.dram_tensor(nm, shp, F32, kind="ExternalInput").ap()
    Bv = {}
    for nm, n in [("bq", D), ("bk", D), ("bv", D), ("bo", D),
                  ("gq", D), ("gk", D), ("b12", FF), ("b3", D)]:
        Bv[nm] = nc.dram_tensor(nm, [n], F32, kind="ExternalInput").ap()
    y_out = nc.dram_tensor("y", [T, D], F32, kind="ExternalOutput").ap()

    sk = "ExternalOutput" if dbg else "Internal"
    ln1f = nc.dram_tensor("ln1f", [NCH, 128, S], BF16, kind=sk).ap()
    kf = nc.dram_tensor("kf", [NCH, 128, S], BF16, kind=sk).ap()
    qf = nc.dram_tensor("qf", [NCH, 128, T], BF16, kind=sk).ap()
    vt = nc.dram_tensor("vt", [H, S, 128], BF16, kind=sk).ap()
    off = nc.dram_tensor("of", [NCH, 128, T], BF16, kind=sk).ap()
    projf = nc.dram_tensor("projf", [NCH, 128, T], BF16, kind=sk).ap()
    x2f = nc.dram_tensor("x2f", [NCH, 128, T], F32, kind=sk).ap()

    id_bf_h = nc.inline_tensor(np.eye(128, dtype=ml_dtypes.bfloat16), "idbf")
    id_f_h = nc.inline_tensor(np.eye(128, dtype=np.float32), "idf")
    onsq_h = nc.inline_tensor(np.ones((128, 128), dtype=ml_dtypes.bfloat16), "onsq")
    onrb_h = nc.inline_tensor(np.ones((1, 128), dtype=ml_dtypes.bfloat16), "onrb")
    onrf_h = nc.inline_tensor(np.ones((1, 128), dtype=np.float32), "onrf")
    oncb_h = nc.inline_tensor(np.ones((128, 1), dtype=ml_dtypes.bfloat16), "oncb")

    A = nc.alloc_sbuf_tensor("bufA", [128, 16384], BF16)[:]
    B = nc.alloc_sbuf_tensor("bufB", [128, 16384], BF16)[:]
    CA = nc.alloc_sbuf_tensor("actA", [128, 16384], BF16)[:]
    CB = nc.alloc_sbuf_tensor("actB", [128, 16384], BF16)[:]
    STG = nc.alloc_sbuf_tensor("stage", [128, 16384], BF16)[:]
    FB = nc.alloc_sbuf_tensor("fbuf", [128, 4096], F32)[:]
    FB2 = nc.alloc_sbuf_tensor("fbuf2", [128, 4096], F32)[:]
    ROWS = nc.alloc_sbuf_tensor("rows", [1, 2564], F32)[:]
    SM = nc.alloc_sbuf_tensor("smalls", [128, 326], F32)[:]
    RBRO = nc.alloc_sbuf_tensor("rowsb", [1, 512], BF16)[:]
    IDB = nc.alloc_sbuf_tensor("identb", [128, 128], BF16)[:]
    IDF = nc.alloc_sbuf_tensor("identf", [128, 128], F32)[:]
    ONSQ = nc.alloc_sbuf_tensor("onessq", [128, 128], BF16)[:]
    ONRB = nc.alloc_sbuf_tensor("onesrowb", [1, 128], BF16)[:]
    ONRF = nc.alloc_sbuf_tensor("onesrowf", [1, 128], F32)[:]
    ONCB = nc.alloc_sbuf_tensor("onescolb", [128, 16], BF16)[:]
    FT = nc.alloc_sbuf_tensor("ftmp", [128, 128], F32)[:]
    FT2 = nc.alloc_sbuf_tensor("ftmp2", [128, 128], F32)[:]

    bqs, bks, bvs, bos = SM[:, 0:32], SM[:, 32:64], SM[:, 64:96], SM[:, 96:128]
    b3s, gqs, gks = SM[:, 128:160], SM[:, 160:192], SM[:, 192:224]
    b12s = SM[:, 224:288]
    STAT = SM[:, 288:304]
    EPS5 = SM[:, 304:305]
    MROW = SM[:, 305:313]
    NM4 = SM[:, 313:317]

    RA, RB_ = ROWS[0:1, 0:512], ROWS[0:1, 512:1024]
    RC, RD = ROWS[0:1, 1024:1536], ROWS[0:1, 1536:2048]
    RT = ROWS[0:1, 2048:2560]
    CEL5 = ROWS[0:1, 2560:2561]
    CEL6 = ROWS[0:1, 2561:2562]

    pb = [nc.alloc_psum_tensor("pb%d" % i, [128, 512], F32)[:] for i in range(6)]
    pb6 = nc.alloc_psum_tensor("pb6", [128, 1024], BF16)[:]
    pb7 = nc.alloc_psum_tensor("pb7", [128, 1024], BF16)[:]
    bsl = [Slot() for _ in range(6)]
    bsl6, bsl7 = Slot(), Slot()
    evb = [(pb6, bsl6), (pb7, bsl7)]

    def v3(ap, c=32):
        return ap.rearrange("p (c t) -> p c t", c=c)

    def ln_finalize(sr, s2r, eps_cell):
        tv = [pl.tok("v"), pl.tok("t"), pl.tok("a")]
        ta = pl.op("vector", lambda E, sr=sr: E.tensor_scalar_mul(sr, sr, R4096),
                   waits=tv, inc="v")
        tb2 = pl.op("vector", lambda E, s2r=s2r: E.tensor_scalar_mul(s2r, s2r, R4096),
                    inc="v")
        tc = pl.op("vector", lambda E, sr=sr: E.tensor_tensor(RT, sr, sr, AL.mult),
                   waits=[ta], inc="v")
        t2 = pl.op("vector", lambda E, s2r=s2r: E.tensor_tensor(s2r, s2r, RT, AL.subtract),
                   waits=[tb2, tc], inc="v")
        t3 = pl.op("scalar", lambda E, s2r=s2r, eps_cell=eps_cell: E.activation(
            RT, s2r, AF.Sqrt, bias=eps_cell, scale=1.0), waits=[t2], inc="a")
        t4 = pl.op("vector", lambda E, s2r=s2r: E.reciprocal(s2r, RT), waits=[t3], inc="v")
        t5 = pl.op("vector", lambda E, sr=sr, s2r=s2r: E.tensor_tensor(RT, sr, s2r, AL.mult),
                   waits=[t4, ta], inc="v")
        return t4, t5

    def bcast_row(src, tok_src, dst):
        tok_bc = pl.op("tensor", MM(pb[0], ONRF, src, True, True),
                       waits=[tok_src] + bsl[0].war_waits(), inc="t")
        bsl[0].produce(tok_bc)
        tok_cp = pl.op("vector", lambda E, dst=dst: E.tensor_copy(dst, pb[0]),
                       waits=[tok_bc], inc="v")
        bsl[0].use(tok_cp)
        return tok_cp

    pl.dma("gpsimd", IDB, id_bf_h.ap())
    pl.dma("gpsimd", IDF, id_f_h.ap())
    pl.dma("gpsimd", ONSQ, onsq_h.ap())
    pl.dma("gpsimd", ONRB, onrb_h.ap())
    pl.dma("gpsimd", ONRF, onrf_h.ap())
    pl.dma("gpsimd", ONCB[:, 0:1], oncb_h.ap())
    for nm, apd in [("bq", bqs), ("bk", bks), ("bv", bvs), ("bo", bos),
                    ("b3", b3s), ("gq", gqs), ("gk", gks)]:
        pl.dma("gpsimd", apd, Bv[nm].rearrange("(c p) -> p c", p=128), slow=True)
    pl.dma("gpsimd", b12s, Bv["b12"].rearrange("(c p) -> p c", p=128), slow=True)
    t_ld0 = pl.tok("ld")
    pl.op("vector", lambda E: E.memset(EPS5, EPS_LN))
    pl.op("vector", lambda E: E.memset(CEL5, EPS_LN))
    pl.op("vector", lambda E: E.memset(CEL6, EPS_RMS))
    pl.op("vector", lambda E: E.tensor_scalar_mul(gqs, gqs, ISQ), waits=[t_ld0], inc="v")
    pl.barrier()

    xt_s = [Slot(), Slot()]
    lt_s = [Slot(), Slot()]
    stg_s = Slot()
    for ts in range(32):
        xt = FB if ts % 2 == 0 else FB2
        xsl = xt_s[ts % 2]
        tok_x = pl.dma("gpsimd", xt, x_in[ts * 128:(ts + 1) * 128, :],
                       waits=xsl.war_waits())
        xsl.produce(tok_x)
        tok_s1 = pl.op("vector", lambda E, xt=xt: E.reduce_sum(STAT[:, 0:1], xt, axis=AX),
                       waits=[tok_x], inc="v")
        tok_sq = pl.op("scalar", lambda E, xt=xt: E.activation(
            CB[:, 0:4096], xt, AF.Square, bias=0.0, scale=1.0,
            accum_out=STAT[:, 1:2]), waits=[tok_x], inc="a")
        xsl.use(tok_sq)
        tok_mn = pl.op("vector", lambda E: E.tensor_scalar_mul(STAT[:, 2:3], STAT[:, 0:1], R4096),
                       waits=[tok_s1], inc="v")
        tok_m2 = pl.op("vector", lambda E: E.tensor_tensor(
            STAT[:, 3:4], STAT[:, 2:3], STAT[:, 2:3], AL.mult), waits=[tok_mn], inc="v")
        tok_var = pl.op("vector", lambda E: E.tensor_scalar(
            STAT[:, 4:5], STAT[:, 1:2], R4096, STAT[:, 3:4], AL.mult, AL.subtract),
            waits=[tok_sq, tok_m2], inc="v")
        tok_sd = pl.op("scalar", lambda E: E.activation(
            STAT[:, 5:6], STAT[:, 4:5], AF.Sqrt, bias=EPS5[:, 0:1], scale=1.0),
            waits=[tok_var], inc="a")
        tok_rc1 = pl.op("vector", lambda E: E.reciprocal(STAT[:, 6:7], STAT[:, 5:6]),
                        waits=[tok_sd], inc="v")
        tok_mr = pl.op("vector", lambda E: E.tensor_tensor(
            STAT[:, 7:8], STAT[:, 2:3], STAT[:, 6:7], AL.mult), waits=[tok_rc1], inc="v")
        lt = CA[:, 0:4096] if ts % 2 == 0 else CA[:, 4096:8192]
        lsl = lt_s[ts % 2]
        tok_lt = pl.op("vector", lambda E, xt=xt, lt=lt: E.tensor_scalar(
            lt, xt, STAT[:, 6:7], STAT[:, 7:8], AL.mult, AL.subtract),
            waits=lsl.war_waits() + [tok_mr], inc="v")
        lsl.produce(tok_lt)
        xsl.use(tok_lt)
        for dc in range(32):
            bank, ebs = evb[dc % 2]
            tok_tr = pl.op("tensor", TR(bank[:, 0:128], lt[:, dc * 128:(dc + 1) * 128], IDB),
                           waits=ebs.war_waits() + [tok_lt], inc="t")
            lsl.use(tok_tr)
            ebs.produce(tok_tr)
            c0 = dc * 512 + (ts % 4) * 128
            tok_ev = pl.op("scalar", lambda E, c0=c0, bank=bank: E.copy(
                STG[:, c0:c0 + 128], bank[:, 0:128]),
                waits=[tok_tr] + ([stg_s.ready] if stg_s.ready else []), inc="a")
            ebs.use(tok_ev)
            stg_s.use(tok_ev)
        if ts % 4 == 3:
            g = ts // 4
            tok_sp = pl.dma("sync",
                            ln1f[:, :, g * 512:(g + 1) * 512].rearrange("c p t -> p c t"),
                            v3(STG), waits=stg_s.consumers)
            stg_s.produce(tok_sp)
    pl.barrier()

    wb_s = [Slot(), Slot()]
    act_s = [Slot(), Slot()]
    kh_s = [Slot(), Slot()]
    sq_s = [Slot(), Slot()]
    vst_s = [Slot(), Slot()]

    def proj(nm, Whbm, bias, gain, out_sp, ntb, rk_dst):
        is_v = nm == "v"
        for tb in range(ntb):
            tA = pl.dma("gpsimd", v3(CA),
                        ln1f[:, :, tb * 1024:tb * 1024 + 512].rearrange("c p t -> p c t"),
                        waits=act_s[0].war_waits())
            act_s[0].produce(tA)
            tB = pl.dma("gpsimd", v3(CB),
                        ln1f[:, :, tb * 1024 + 512:tb * 1024 + 1024].rearrange("c p t -> p c t"),
                        waits=act_s[1].war_waits())
            act_s[1].produce(tB)
            if not is_v:
                ztoks = [pl.op("vector", lambda E, r=r: E.memset(r, 0.0),
                               waits=[pl.tok("t")], inc="v") for r in (RA, RB_)]
                acc_tok = {0: ztoks[0], 1: ztoks[1]}
            for ob in range(8):
                wb = A if ob % 2 == 0 else B
                wsl = wb_s[ob % 2]
                tW = pl.dma("gpsimd", v3(wb),
                            Whbm.rearrange("(c p) o -> p c o", p=128)[:, :, ob * 512:(ob + 1) * 512],
                            waits=wsl.war_waits())
                wsl.produce(tW)
                for oc in range(4):
                    ocg = ob * 4 + oc
                    for tg in range(2):
                        bi = (oc * 2 + tg) % 4
                        bank = pb[bi]
                        act = CA if tg == 0 else CB
                        tok_mm = None
                        for dc in range(32):
                            w0 = (bsl[bi].war_waits() + [tW, tA if tg == 0 else tB]) if dc == 0 else []
                            tok_mm = pl.op("tensor", MM(
                                bank, v3(wb)[:, dc, oc * 128:(oc + 1) * 128],
                                v3(act)[:, dc, :], dc == 0, dc == 31),
                                waits=w0, inc="t" if dc == 31 else None)
                        bsl[bi].produce(tok_mm)
                        wsl.use(tok_mm)
                        act_s[tg].use(tok_mm)
                        if is_v:
                            vslot = STG[:, 12288 + tg * 512:12288 + tg * 512 + 512]
                            tok_vv = pl.op("vector", lambda E, bank=bank, vslot=vslot, ocg=ocg:
                                           E.tensor_scalar_add(vslot, bank, bvs[:, ocg:ocg + 1]),
                                           waits=[tok_mm], inc="v")
                            bsl[bi].use(tok_vv)
                            vstg = STG[:, 8192 + (oc % 2) * 1024:8192 + (oc % 2) * 1024 + 1024]
                            vsl = vst_s[oc % 2]
                            for j in range(4):
                                tbank, ebs = evb[j % 2]
                                tok_tr = pl.op("tensor", TR(
                                    tbank[:, 0:128], vslot[:, j * 128:(j + 1) * 128], IDB),
                                    waits=ebs.war_waits() + [tok_vv], inc="t")
                                ebs.produce(tok_tr)
                                blk = tg * 4 + j
                                tok_ev = pl.op("scalar", lambda E, vstg=vstg, blk=blk, tbank=tbank:
                                               E.copy(vstg[:, blk * 128:(blk + 1) * 128], tbank[:, 0:128]),
                                               waits=[tok_tr] + (vsl.war_waits() if (tg == 0 and j == 0) else []),
                                               inc="a")
                                ebs.use(tok_ev)
                                vsl.use(tok_ev)
                            if tg == 1:
                                tok_sp = pl.dma("sync",
                                                vt[ocg, tb * 1024:(tb + 1) * 1024, :]
                                                .rearrange("(blk p) d -> p blk d", p=128),
                                                vstg.rearrange("p (blk d) -> p blk d", blk=8),
                                                waits=vsl.consumers)
                                vsl.produce(tok_sp)
                        else:
                            sqs = STG[:, 8192 + tg * 512:8192 + tg * 512 + 512]
                            tok_sq = pl.op("scalar", lambda E, bank=bank, sqs=sqs, ocg=ocg, bias=bias:
                                           E.activation(sqs, bank, AF.Square,
                                                        bias=bias[:, ocg:ocg + 1], scale=1.0),
                                           waits=[tok_mm] + sq_s[tg].war_waits(), inc="a")
                            sq_s[tg].produce(tok_sq)
                            c0 = (ob % 2) * 4096 + oc * 1024 + tg * 512
                            kslot = STG[:, c0:c0 + 512]
                            tok_kh = pl.op("vector", lambda E, bank=bank, kslot=kslot, ocg=ocg,
                                           bias=bias, gain=gain: E.tensor_scalar(
                                               kslot, bank, bias[:, ocg:ocg + 1],
                                               gain[:, ocg:ocg + 1], AL.add, AL.mult),
                                           waits=[tok_sq] + (kh_s[ob % 2].war_waits()
                                                             if (oc == 0 and tg == 0) else []),
                                           inc="v")
                            bsl[bi].use(tok_kh)
                            kh_s[ob % 2].use(tok_kh)
                            vbi = 4 + tg
                            tok_vm = pl.op("tensor", MM(pb[vbi][0:1, :], ONCB[:, 0:1], sqs, True, True),
                                           waits=[tok_sq] + bsl[vbi].war_waits(), inc="t")
                            bsl[vbi].produce(tok_vm)
                            sq_s[tg].use(tok_vm)
                            rr = RA if tg == 0 else RB_
                            tok_va = pl.op("vector", lambda E, rr=rr, vbi=vbi: E.tensor_tensor(
                                rr, rr, pb[vbi][0:1, :], AL.add),
                                waits=[tok_vm, acc_tok[tg]] + ztoks, inc="v")
                            acc_tok[tg] = tok_va
                            bsl[vbi].use(tok_va)
                if not is_v:
                    reg = STG[:, (ob % 2) * 4096:(ob % 2) * 4096 + 4096]
                    tok_sp = pl.dma("sync",
                                    out_sp[ob * 4:(ob + 1) * 4, :, tb * 1024:(tb + 1) * 1024]
                                    .rearrange("c p t -> p c t"),
                                    reg.rearrange("p (c t) -> p c t", c=4),
                                    waits=kh_s[ob % 2].consumers)
                    kh_s[ob % 2].produce(tok_sp)
            if not is_v:
                for tg in range(2):
                    rr = RA if tg == 0 else RB_
                    t1 = pl.op("vector", lambda E, rr=rr: E.tensor_scalar(
                        RT, rr, R4096, CEL6[0:1, 0:1], AL.mult, AL.add),
                        waits=[acc_tok[tg], pl.tok("t")], inc="v")
                    t2 = pl.op("scalar", lambda E: E.activation(
                        RC, RT, AF.Sqrt, bias=0.0, scale=1.0), waits=[t1], inc="a")
                    t3 = pl.op("vector", lambda E: E.reciprocal(RT, RC), waits=[t2], inc="v")
                    dst = rk_dst[:, tb * 1024 + tg * 512:tb * 1024 + tg * 512 + 512]
                    bcast_row(RT, t3, dst)

    proj("k", W["Wk"], bks, gks, kf, 4, FB)
    pl.barrier()
    proj("v", W["Wv"], bvs, None, None, 4, None)
    pl.barrier()
    proj("q", W["Wq"], bqs, gqs, qf, 2, FB2[:, 0:2048])
    pl.barrier()

    kraw = [A[:, 0:4096], A[:, 4096:8192]]
    ksc = [A[:, 8192:12288], A[:, 12288:16384]]
    vbuf = [CA[:, 0:4096], CB[:, 0:4096]]
    qraw = [STG[:, 0:2048], STG[:, 2048:4096]]
    qsc = [STG[:, 4096:6144], STG[:, 6144:8192]]
    ptile = [STG[:, 8192 + i * 512:8192 + (i + 1) * 512] for i in range(3)]
    diag4 = STG[:, 9728:10240]
    recipb = STG[:, 10240:10752]
    obf = [STG[:, 10752 + i * 512:10752 + (i + 1) * 512] for i in range(2)]
    kraw_s = [Slot(), Slot()]
    ksc_s = [Slot(), Slot()]
    vb_s = [Slot(), Slot()]
    qraw_s = [Slot(), Slot()]
    qsc_s = [Slot(), Slot()]
    pt_s = [Slot(), Slot(), Slot()]
    rec_s = Slot()
    obf_s = [Slot(), Slot()]
    rbro_s = Slot()
    prev_mb = None

    for h in range(H):
        par = h % 2
        tok_k = pl.dma("gpsimd", kraw[par], kf[h], waits=kraw_s[par].war_waits())
        kraw_s[par].produce(tok_k)
        tok_v = pl.dma("gpsimd", v3(vbuf[par]),
                       vt[h].rearrange("(c p) d -> p c d", p=128),
                       waits=vb_s[par].war_waits())
        vb_s[par].produce(tok_v)
        tok_q = pl.dma("gpsimd", qraw[par], qf[h], waits=qraw_s[par].war_waits())
        qraw_s[par].produce(tok_q)
        tok_ks = pl.op("vector", lambda E, p=par: E.tensor_tensor(ksc[p], kraw[p], FB, AL.mult),
                       waits=[tok_k] + ksc_s[par].war_waits(), inc="v")
        ksc_s[par].produce(tok_ks)
        kraw_s[par].use(tok_ks)
        tok_qs = pl.op("vector", lambda E, p=par: E.tensor_tensor(
            qsc[p], qraw[p], FB2[:, 0:2048], AL.mult),
            waits=[tok_q] + qsc_s[par].war_waits(), inc="v")
        qsc_s[par].produce(tok_qs)
        qraw_s[par].use(tok_qs)
        kh, qh, vh = ksc[par], qsc[par], v3(vbuf[par])
        for qt in range(4):
            tok_dg = None
            for j in range(4):
                q0 = qt * 512 + j * 128
                for kc in range(8):
                    bi = kc % 4
                    tok_mm = pl.op("tensor", MM(pb[bi], qh[:, q0:q0 + 128],
                                                kh[:, kc * 512:(kc + 1) * 512], True, True),
                                   waits=bsl[bi].war_waits() + [tok_qs, tok_ks], inc="t")
                    bsl[bi].produce(tok_mm)
                    qsc_s[par].use(tok_mm)
                    ksc_s[par].use(tok_mm)
                    tok_rd = pl.op("vector", lambda E, bi=bi, kc=kc: E.reduce_max(
                        MROW[:, kc:kc + 1], pb[bi], axis=AX), waits=[tok_mm], inc="v")
                    bsl[bi].use(tok_rd)
                tok_nm = pl.op("vector", lambda E, j=j: E.reduce_max(
                    NM4[:, j:j + 1], MROW[:, 0:8], axis=AX, negate=True),
                    waits=[tok_rd], inc="v")
                tok_dg = pl.op("vector", lambda E, j=j: E.tensor_scalar_mul(
                    diag4[:, j * 128:(j + 1) * 128], IDB, NM4[:, j:j + 1]),
                    waits=[tok_nm] + ([prev_mb] if (j == 0 and prev_mb) else []), inc="v")
            tok_diag = tok_dg
            tok_sum31 = tok_pv31 = None

            def emit_sumpv(kcp, tok_ep):
                tok_sum = pl.op("tensor", MM(pb[4][0:1, :], ONCB[:, 0:1], ptile[kcp % 3],
                                             kcp == 0, kcp == 31),
                                waits=[tok_ep] + (bsl[4].war_waits() if kcp == 0 else []),
                                inc="t" if kcp == 31 else None)
                tok_pv = pl.op("tensor", MM(pb[5], vh[:, kcp, :], ptile[kcp % 3],
                                            kcp == 0, kcp == 31),
                               waits=[tok_ep, tok_v] + (bsl[5].war_waits() if kcp == 0 else []),
                               inc="t" if kcp == 31 else None)
                pt_s[kcp % 3].use(tok_sum)
                pt_s[kcp % 3].use(tok_pv)
                vb_s[par].use(tok_pv)
                return tok_sum, tok_pv

            pend = None
            for kc in range(32):
                bi = 2 + kc % 2
                bank = pb[bi]
                tok_st = pl.op("tensor", MM(bank, kh[:, kc * 128:(kc + 1) * 128],
                                            qh[:, qt * 512:(qt + 1) * 512], True, False),
                               waits=bsl[bi].war_waits())
                tok_mb = pl.op("tensor", MM(bank, ONSQ, diag4, False, True),
                               waits=[tok_diag], inc="t")
                bsl[bi].produce(tok_mb)
                ksc_s[par].use(tok_mb)
                qsc_s[par].use(tok_mb)
                prev_mb = tok_mb
                psl = pt_s[kc % 3]
                tok_e = pl.op("scalar", lambda E, bank=bank, kc=kc: E.activation(
                    ptile[kc % 3], bank, AF.Exp, bias=0.0, scale=1.0),
                    waits=[tok_mb] + psl.war_waits(), inc="a")
                bsl[bi].use(tok_e)
                psl.produce(tok_e)
                if pend is not None:
                    emit_sumpv(*pend)
                pend = (kc, tok_e)
            tok_sum31, tok_pv31 = emit_sumpv(*pend)
            bsl[4].produce(tok_sum31)
            bsl[5].produce(tok_pv31)
            tok_r = pl.op("vector", lambda E: E.reciprocal(RA, pb[4][0:1, :]),
                          waits=[tok_sum31], inc="v")
            bsl[4].use(tok_r)
            tok_rb = pl.op("vector", lambda E: E.tensor_copy(RBRO, RA),
                           waits=[tok_r] + rbro_s.war_waits(), inc="v")
            tok_bc = pl.op("tensor", MM(pb[4], ONRB, RBRO, True, True),
                           waits=[tok_rb, tok_r], inc="t")
            bsl[4].produce(tok_bc)
            rbro_s.produce(tok_rb)
            rbro_s.use(tok_bc)
            tok_rc = pl.op("vector", lambda E: E.tensor_copy(recipb, pb[4]),
                           waits=[tok_bc] + rec_s.war_waits(), inc="v")
            bsl[4].use(tok_rc)
            rec_s.produce(tok_rc)
            osl = obf_s[qt % 2]
            tok_o = pl.op("vector", lambda E, qt=qt: E.tensor_tensor(
                obf[qt % 2], pb[5], recipb, AL.mult),
                waits=[tok_pv31, tok_rc] + osl.war_waits(), inc="v")
            bsl[5].use(tok_o)
            rec_s.use(tok_o)
            osl.produce(tok_o)
            tok_os = pl.dma("sync", off[h, :, qt * 512:(qt + 1) * 512], obf[qt % 2],
                            waits=[tok_o])
            osl.use(tok_os)
    pl.barrier()

    pjload_tok = {}
    ln3bc = []
    for tb in range(2):
        tA = pl.dma("gpsimd", v3(CA), off[:, :, tb * 1024:tb * 1024 + 512]
                    .rearrange("c p t -> p c t"), waits=act_s[0].war_waits())
        act_s[0].produce(tA)
        tB = pl.dma("gpsimd", v3(CB), off[:, :, tb * 1024 + 512:tb * 1024 + 1024]
                    .rearrange("c p t -> p c t"), waits=act_s[1].war_waits())
        act_s[1].produce(tB)
        zt = [pl.op("vector", lambda E, r=r: E.memset(r, 0.0),
                    waits=[pl.tok("t")], inc="v") for r in (RA, RB_, RC, RD)]
        acc4 = {(0, 0): zt[0], (1, 0): zt[1], (0, 1): zt[2], (1, 1): zt[3]}
        prj_s = [Slot(), Slot()]
        for ob in range(8):
            wb = A if ob % 2 == 0 else B
            wsl = wb_s[ob % 2]
            tW = pl.dma("gpsimd", v3(wb),
                        W["Wo"].rearrange("(c p) o -> p c o", p=128)[:, :, ob * 512:(ob + 1) * 512],
                        waits=wsl.war_waits())
            wsl.produce(tW)
            for oc in range(4):
                ocg = ob * 4 + oc
                for tg in range(2):
                    bi = (oc * 2 + tg) % 4
                    bank = pb[bi]
                    act = CA if tg == 0 else CB
                    tok_mm = None
                    for ic in range(32):
                        w0 = (bsl[bi].war_waits() + [tW, tA if tg == 0 else tB]) if ic == 0 else []
                        tok_mm = pl.op("tensor", MM(bank, v3(wb)[:, ic, oc * 128:(oc + 1) * 128],
                                                    v3(act)[:, ic, :], ic == 0, ic == 31),
                                       waits=w0, inc="t" if ic == 31 else None)
                    bsl[bi].produce(tok_mm)
                    wsl.use(tok_mm)
                    act_s[tg].use(tok_mm)
                    c0 = (ob % 2) * 4096 + oc * 1024 + tg * 512
                    zslot = STG[:, c0:c0 + 512]
                    tok_z = pl.op("scalar", lambda E, bank=bank, zslot=zslot, ocg=ocg:
                                  E.activation(zslot, bank, AF.Identity,
                                               bias=bos[:, ocg:ocg + 1], scale=1.0),
                                  waits=[tok_mm] + (prj_s[ob % 2].war_waits()
                                                    if (oc == 0 and tg == 0) else []), inc="a")
                    prj_s[ob % 2].use(tok_z)
                    sqs = STG[:, 8192 + tg * 512:8192 + tg * 512 + 512]
                    tok_sq = pl.op("scalar", lambda E, bank=bank, sqs=sqs, ocg=ocg:
                                   E.activation(sqs, bank, AF.Square,
                                                bias=bos[:, ocg:ocg + 1], scale=1.0),
                                   waits=[tok_mm] + sq_s[tg].war_waits(), inc="a")
                    sq_s[tg].produce(tok_sq)
                    bsl[bi].use(tok_sq)
                    for k2 in range(2):
                        rhs = zslot if k2 == 0 else sqs
                        tok_rhs = tok_z if k2 == 0 else tok_sq
                        rr = (RA, RB_)[tg] if k2 == 0 else (RC, RD)[tg]
                        vbi = 4 + k2
                        tok_sm = pl.op("tensor", MM(pb[vbi][0:1, :], ONCB[:, 0:1], rhs, True, True),
                                       waits=[tok_rhs] + bsl[vbi].war_waits(), inc="t")
                        bsl[vbi].produce(tok_sm)
                        if k2 == 0:
                            prj_s[ob % 2].use(tok_sm)
                        else:
                            sq_s[tg].use(tok_sm)
                        tok_sa = pl.op("vector", lambda E, rr=rr, vbi=vbi: E.tensor_tensor(
                            rr, rr, pb[vbi][0:1, :], AL.add),
                            waits=[tok_sm, acc4[(tg, k2)]], inc="v")
                        acc4[(tg, k2)] = tok_sa
                        bsl[vbi].use(tok_sa)
            reg = STG[:, (ob % 2) * 4096:(ob % 2) * 4096 + 4096]
            tok_sp = pl.dma("sync",
                            projf[ob * 4:(ob + 1) * 4, :, tb * 1024:(tb + 1) * 1024]
                            .rearrange("c p t -> p c t"),
                            reg.rearrange("p (c t) -> p c t", c=4),
                            waits=prj_s[ob % 2].consumers)
            prj_s[ob % 2].produce(tok_sp)
            pjload_tok[(tb, ob)] = tok_sp
        ln2bc = []
        for tg in range(2):
            sr, s2r = (RA, RB_)[tg], (RC, RD)[tg]
            ti, tm = ln_finalize(sr, s2r, CEL5[0:1, 0:1])
            ln2bc.append(bcast_row(s2r, ti, FB[:, tb * 1024 + tg * 512:tb * 1024 + tg * 512 + 512]))
            ln2bc.append(bcast_row(RT, tm, FB[:, 2048 + tb * 1024 + tg * 512:2048 + tb * 1024 + tg * 512 + 512]))
        zt3 = [pl.op("vector", lambda E, r=r: E.memset(r, 0.0),
                     waits=[pl.tok("t"), pl.tok("v")], inc="v")
               for r in (RA, RB_, RC, RD)]
        acc3 = {(0, 0): zt3[0], (1, 0): zt3[1], (0, 1): zt3[2], (1, 1): zt3[3]}
        xtm_s = Slot()
        pj_s = [Slot(), Slot()]
        x2b_s = [Slot(), Slot()]
        sq3_s = [Slot(), Slot()]
        xf_s = Slot()
        x2_s = Slot()
        for dc in range(32):
            tok_xm = pl.dma("gpsimd", FB2[:, 0:1024].rearrange("p (blk d) -> p blk d", blk=8),
                            x_in[tb * 1024:(tb + 1) * 1024, dc * 128:(dc + 1) * 128]
                            .rearrange("(blk p) d -> p blk d", p=128),
                            waits=xtm_s.war_waits())
            xtm_s.produce(tok_xm)
            pjt = STG[:, 9216 + (dc % 2) * 1024:9216 + (dc % 2) * 1024 + 1024]
            tok_pj = pl.dma("gpsimd", pjt, projf[dc][:, tb * 1024:(tb + 1) * 1024],
                            waits=pj_s[dc % 2].war_waits() + [pjload_tok[(tb, dc // 4)]])
            pj_s[dc % 2].produce(tok_pj)
            tok_ev = None
            trtoks = [None] * 9
            for blk in range(8):
                bi = blk % 2
                tok_tr = pl.op("tensor", TR(pb[bi][:, 0:128],
                                            FB2[:, blk * 128:(blk + 1) * 128], IDF),
                               waits=bsl[bi].war_waits() + [tok_xm], inc="t")
                trtoks[blk] = tok_tr
                bsl[bi].produce(tok_tr)
                xtm_s.use(tok_tr)
                if blk > 0:
                    tok_ev = pl.op("vector", lambda E, blk=blk, bi=bi: E.tensor_copy(
                        FB2[:, 1024 + (blk - 1) * 128:1024 + blk * 128], pb[1 - bi][:, 0:128]),
                        waits=[tok_tr] + (xf_s.war_waits() if blk == 1 else []), inc="v")
                    bsl[1 - bi].use(tok_ev)
    
            tok_dmy = pl.op("tensor", TR(pb[0][:, 0:128],
                                         FB2[:, 7 * 128:8 * 128], IDF),
                            waits=bsl[0].war_waits(), inc="t")
            bsl[0].produce(tok_dmy)
            tok_ev = pl.op("vector", lambda E: E.tensor_copy(
                FB2[:, 1024 + 7 * 128:1024 + 8 * 128], pb[1][:, 0:128]),
                waits=[tok_dmy], inc="v")
            bsl[1].use(tok_ev)
            bsl[0].use(tok_ev)
            tok_t1 = pl.op("vector", lambda E, pjt=pjt, tb=tb: E.tensor_tensor(
                FB2[:, 2048:3072], pjt, FB[:, tb * 1024:(tb + 1) * 1024], AL.mult),
                waits=[tok_pj] + ln2bc + x2_s.war_waits(), inc="v")
            pj_s[dc % 2].use(tok_t1)
            tok_t2 = pl.op("vector", lambda E, tb=tb: E.tensor_tensor(
                FB2[:, 3072:4096], FB2[:, 2048:3072],
                FB[:, 2048 + tb * 1024:2048 + (tb + 1) * 1024], AL.subtract),
                waits=[tok_t1], inc="v")
            tok_x2 = pl.op("vector", lambda E: E.tensor_tensor(
                FB2[:, 2048:3072], FB2[:, 3072:4096], FB2[:, 1024:2048], AL.add),
                waits=[tok_ev, tok_t2], inc="v")
            xf_s.produce(tok_x2)
            x2_s.produce(tok_x2)
            tok_xsp = pl.dma("sync", x2f[dc][:, tb * 1024:(tb + 1) * 1024],
                             FB2[:, 2048:3072], waits=[tok_x2])
            x2_s.use(tok_xsp)
            x2bf = STG[:, 11264 + (dc % 2) * 1024:11264 + (dc % 2) * 1024 + 1024]
            tok_xb = pl.op("scalar", lambda E, x2bf=x2bf: E.activation(
                x2bf, FB2[:, 2048:3072], AF.Identity, bias=0.0, scale=1.0),
                waits=[tok_x2] + x2b_s[dc % 2].war_waits(), inc="a")
            x2b_s[dc % 2].produce(tok_xb)
            x2_s.use(tok_xb)
            sq3 = STG[:, 13312 + (dc % 2) * 1024:13312 + (dc % 2) * 1024 + 1024]
            tok_s3 = pl.op("scalar", lambda E, sq3=sq3: E.activation(
                sq3, FB2[:, 2048:3072], AF.Square, bias=0.0, scale=1.0),
                waits=[tok_x2] + sq3_s[dc % 2].war_waits(), inc="a")
            sq3_s[dc % 2].produce(tok_s3)
            x2_s.use(tok_s3)
            for tg in range(2):
                for k2 in range(2):
                    rhs = (x2bf if k2 == 0 else sq3)[:, tg * 512:(tg + 1) * 512]
                    tok_rhs = tok_xb if k2 == 0 else tok_s3
                    rr = (RA, RB_)[tg] if k2 == 0 else (RC, RD)[tg]
                    vbi = 4 + k2
                    tok_sm = pl.op("tensor", MM(pb[vbi][0:1, :], ONCB[:, 0:1], rhs, True, True),
                                   waits=[tok_rhs] + bsl[vbi].war_waits(), inc="t")
                    bsl[vbi].produce(tok_sm)
                    (x2b_s if k2 == 0 else sq3_s)[dc % 2].use(tok_sm)
                    tok_sa = pl.op("vector", lambda E, rr=rr, vbi=vbi: E.tensor_tensor(
                        rr, rr, pb[vbi][0:1, :], AL.add),
                        waits=[tok_sm, acc3[(tg, k2)]], inc="v")
                    acc3[(tg, k2)] = tok_sa
                    bsl[vbi].use(tok_sa)
        for tg in range(2):
            sr, s2r = (RA, RB_)[tg], (RC, RD)[tg]
            ti, tm = ln_finalize(sr, s2r, CEL5[0:1, 0:1])
            ln3bc.append(bcast_row(s2r, ti, FB[:, tb * 1024 + tg * 512:tb * 1024 + tg * 512 + 512]))
            ln3bc.append(bcast_row(RT, tm, FB[:, 2048 + tb * 1024 + tg * 512:2048 + tb * 1024 + tg * 512 + 512]))
        pl.barrier()

    ln3_s = Slot()
    si_s = Slot()
    af_s = Slot()
    x2l_s = [Slot(), Slot()]
    for t5 in range(4):
        xl3 = FB2[:, 2048:4096].rearrange("p (c t) -> p c t", c=4)
        for dc in range(32):
            xsl = x2l_s[0]
            if dc % 4 == 0:
                tok_xl = pl.dma("gpsimd", xl3,
                                x2f[dc:dc + 4, :, t5 * 512:(t5 + 1) * 512]
                                .rearrange("c p t -> p c t"),
                                waits=xsl.war_waits())
                xsl.produce(tok_xl)
            xslot = xl3[:, dc % 4, :]
            tmp = FB2[:, 1024 + (dc % 2) * 512:1024 + (dc % 2) * 512 + 512]
            tok_m = pl.op("vector", lambda E, xslot=xslot, tmp=tmp, t5=t5: E.tensor_tensor(
                tmp, xslot, FB[:, t5 * 512:(t5 + 1) * 512], AL.mult),
                waits=[tok_xl] + ln3bc, inc="v")
            xsl.use(tok_m)
            dst = v3(CA)[:, dc, :]
            tok_l3 = pl.op("vector", lambda E, dst=dst, tmp=tmp, t5=t5: E.tensor_tensor(
                dst, tmp, FB[:, 2048 + t5 * 512:2048 + (t5 + 1) * 512], AL.subtract),
                waits=[tok_m] + (ln3_s.war_waits() if dc == 0 else []), inc="v")
            ln3_s.use(tok_l3)
            if paranoid & 8:
                pl.barrier()
        tok_ln3 = pl.tok("v")
        for ob in range(16):
            wb = A if ob % 2 == 0 else B
            wsl = wb_s[ob % 2]
            tW = pl.dma("gpsimd", v3(wb),
                        W["w12"].rearrange("(c p) o -> p c o", p=128)[:, :, ob * 512:(ob + 1) * 512],
                        waits=wsl.war_waits())
            wsl.produce(tW)
            for oc in range(4):
                ocg = ob * 4 + oc
                bi = ocg % 4
                bank = pb[bi]
                tok_mm = None
                for dc in range(32):
                    w0 = (bsl[bi].war_waits() + [tW, tok_ln3]) if dc == 0 else []
                    tok_mm = pl.op("tensor", MM(bank, v3(wb)[:, dc, oc * 128:(oc + 1) * 128],
                                                v3(CA)[:, dc, :], dc == 0, dc == 31),
                                   waits=w0, inc="t" if dc == 31 else None)
                bsl[bi].produce(tok_mm)
                wsl.use(tok_mm)
                ln3_s.use(tok_mm)
                if ob < 8:
                    sdst = v3(CB)[:, ocg, :]
                    tok_si = pl.op("scalar", lambda E, bank=bank, sdst=sdst, ocg=ocg:
                                   E.activation(sdst, bank, AF.Silu,
                                                bias=b12s[:, ocg:ocg + 1], scale=1.0),
                                   waits=[tok_mm] + (si_s.war_waits() if ocg == 0 else []),
                                   inc="a")
                    si_s.use(tok_si)
                    bsl[bi].use(tok_si)
                else:
                    mslot = ocg - 32
                    adst = v3(STG)[:, mslot, :]
                    tok_af = pl.op("vector", lambda E, bank=bank, adst=adst, mslot=mslot, ocg=ocg:
                                   E.scalar_tensor_tensor(adst, bank, b12s[:, ocg:ocg + 1],
                                                          v3(CB)[:, mslot, :], AL.add, AL.mult),
                                   waits=[tok_mm, pl.tok("a")] + (af_s.war_waits() if mslot == 0 else []),
                                   inc="v")
                    af_s.use(tok_af)
                    si_s.use(tok_af)
                    bsl[bi].use(tok_af)
                if paranoid & 2:
                    pl.barrier()
        tok_af_all = pl.tok("v")
        tok_w12_t = pl.tok("t")
        zt4 = [pl.op("vector", lambda E, r=r: E.memset(r, 0.0),
                     waits=[pl.tok("v"), pl.tok("t")], inc="v") for r in (RA, RC)]
        accw = {0: zt4[0], 1: zt4[1]}
        fdone = {}
        for ob in range(8):
            wb = A if ob % 2 == 0 else B
            wsl = wb_s[ob % 2]
            tW = pl.dma("gpsimd", v3(wb),
                        W["w3"].rearrange("(c p) o -> p c o", p=128)[:, :, ob * 512:(ob + 1) * 512],
                        waits=wsl.war_waits())
            wsl.produce(tW)
            for oc in range(4):
                ocg = ob * 4 + oc
                bi = ocg % 4
                bank = pb[bi]
                tok_mm = None
                for mc in range(32):
                    w0 = (bsl[bi].war_waits() + [tW, tok_af_all]) if mc == 0 else []
                    tok_mm = pl.op("tensor", MM(bank, v3(wb)[:, mc, oc * 128:(oc + 1) * 128],
                                                v3(STG)[:, mc, :], mc == 0, mc == 31),
                                   waits=w0, inc="t" if mc == 31 else None)
                bsl[bi].produce(tok_mm)
                wsl.use(tok_mm)
                af_s.use(tok_mm)
                fdst = v3(CB)[:, ocg, :]
                tok_f = pl.op("scalar", lambda E, bank=bank, fdst=fdst, ocg=ocg:
                              E.activation(fdst, bank, AF.Identity,
                                           bias=b3s[:, ocg:ocg + 1], scale=1.0),
                              waits=[tok_mm, tok_af_all], inc="a")
                fdone[ocg] = tok_f
                sqs = v3(CA)[:, ocg % 2, :]
                tok_sq = pl.op("scalar", lambda E, bank=bank, sqs=sqs, ocg=ocg:
                               E.activation(sqs, bank, AF.Square,
                                            bias=b3s[:, ocg:ocg + 1], scale=1.0),
                               waits=[tok_mm, tok_w12_t] + sq_s[ocg % 2].war_waits(), inc="a")
                sq_s[ocg % 2].produce(tok_sq)
                bsl[bi].use(tok_sq)
                for k2 in range(2):
                    rhs = fdst if k2 == 0 else sqs
                    tok_rhs = tok_f if k2 == 0 else tok_sq
                    rr = RA if k2 == 0 else RC
                    vbi = 4 + k2
                    tok_sm = pl.op("tensor", MM(pb[vbi][0:1, :], ONCB[:, 0:1], rhs, True, True),
                                   waits=[tok_rhs] + bsl[vbi].war_waits(), inc="t")
                    bsl[vbi].produce(tok_sm)
                    if k2 == 0:
                        fdone[ocg] = tok_f
                    else:
                        sq_s[ocg % 2].use(tok_sm)
                    tok_sa = pl.op("vector", lambda E, rr=rr, vbi=vbi: E.tensor_tensor(
                        rr, rr, pb[vbi][0:1, :], AL.add),
                        waits=[tok_sm, accw[k2]], inc="v")
                    accw[k2] = tok_sa
                    bsl[vbi].use(tok_sa)
                if paranoid & 4:
                    pl.barrier()
        ti, tm = ln_finalize(RA, RC, CEL5[0:1, 0:1])
        bc0 = bcast_row(RC, ti, FB2[:, 1024:1536])
        bc1 = bcast_row(RT, tm, FB2[:, 1536:2048])
        xhalf_s = Slot()
        yst_s = [Slot(), Slot()]
        ft_sl = [Slot(), Slot()]
        for j in range(4):
            t0 = t5 * 512 + j * 128
            def emit_evac_store(dcp, gate_tok):
                bip = dcp % 4
                grp = dcp // 4
                ysl = yst_s[grp % 2]
                ystg4 = FB2[:, (grp % 2) * 512:(grp % 2) * 512 + 512]
                tok_ev = pl.op("vector", lambda E, ystg4=ystg4, dcp=dcp, bip=bip: E.tensor_copy(
                    ystg4[:, (dcp % 4) * 128:(dcp % 4 + 1) * 128], pb[bip][:, 0:128]),
                    waits=[gate_tok] + (ysl.war_waits() if dcp % 4 == 0 else []), inc="v")
                bsl[bip].use(tok_ev)
                ysl.use(tok_ev)
                if dcp % 4 == 3:
                    tok_ys = pl.dma("sync",
                                    y_out[t0:t0 + 128, grp * 512:(grp + 1) * 512],
                                    ystg4,
                                    waits=ysl.consumers + [("sf", pl.cnt["sf"])], sem="sf")
                    ysl.produce(tok_ys)

            xchunk = FB2[:, 2048:4096]
            xc3 = xchunk.rearrange("p (c t) -> p c t", c=16)
            for dc in range(32):
                half = dc // 16
                if dc % 16 == 0:
                    tok_xl = pl.dma("gpsimd", xc3,
                                    x2f[half * 16:(half + 1) * 16, :, t0:t0 + 128]
                                    .rearrange("c p t -> p c t"),
                                    waits=xhalf_s.war_waits() + [tok_af_all])
                    xhalf_s.produce(tok_xl)
                xslot = xc3[:, dc % 16, :]
                ftb = FT if dc % 2 == 0 else FT2
                fts = ft_sl[dc % 2]
                fsl = v3(CB)[:, dc, j * 128:(j + 1) * 128]
                tok_1 = pl.op("vector", lambda E, fsl=fsl, j=j, ftb=ftb: E.tensor_tensor(
                    ftb, fsl, FB2[:, 1024 + j * 128:1024 + (j + 1) * 128], AL.mult),
                    waits=[fdone[dc], bc0, bc1] + fts.war_waits(), inc="v")
                tok_2 = pl.op("vector", lambda E, j=j, ftb=ftb: E.tensor_tensor(
                    ftb, ftb, FB2[:, 1536 + j * 128:1536 + (j + 1) * 128], AL.subtract),
                    waits=[tok_1], inc="v")
                tok_3 = pl.op("vector", lambda E, xslot=xslot, ftb=ftb: E.tensor_tensor(
                    ftb, ftb, xslot, AL.add), waits=[tok_xl, tok_2], inc="v")
                xhalf_s.use(tok_3)
                bi = dc % 4
                tok_tr = pl.op("tensor", TR(pb[bi][:, 0:128], ftb, IDF),
                               waits=bsl[bi].war_waits() + [tok_3], inc="t")
                bsl[bi].produce(tok_tr)
                fts.produce(tok_tr)
                if dc > 1:
                    emit_evac_store(dc - 2, tok_tr)
                if paranoid & 1:
                    pl.barrier()
            for ex in range(2):
                bi = ex % 4
                tok_dmy = pl.op("tensor", TR(pb[4 + ex][:, 0:128], FT, IDF),
                                waits=bsl[4 + ex].war_waits(), inc="t")
                bsl[4 + ex].produce(tok_dmy)
                emit_evac_store(30 + ex, tok_dmy)
                bsl[4 + ex].use(pl.tok("v"))
        pl.barrier()

    pl.op("sync", None, waits=[("sf", pl.cnt["sf"])])

    with nc.Block() as blk:
        blk.tensor(lambda E: pl.replay(E, "tensor"))
        blk.vector(lambda E: pl.replay(E, "vector"))
        blk.scalar(lambda E: pl.replay(E, "scalar"))
        blk.sync(lambda E: pl.replay(E, "sync"))
        blk.gpsimd(lambda E: pl.replay(E, "gpsimd"))
    return nc


def kernel(**inputs):
    x = np.ascontiguousarray(np.asarray(inputs["L0_attn_hidden_states"]))
    wmap = {
        "Wq": inputs["L0_attn_Wq"], "Wk": inputs["L0_attn_Wk"],
        "Wv": inputs["L0_attn_Wv"], "Wo": inputs["L0_attn_Wo"],
        "w12": np.asarray(inputs["L0_ffn_w12"]).T, "w3": np.asarray(inputs["L0_ffn_w3"]).T,
        "bq": inputs["L0_attn_bq"], "bk": inputs["L0_attn_bk"],
        "bv": inputs["L0_attn_bv"], "bo": inputs["L0_attn_bo"],
        "gq": inputs["L0_attn_gq"], "gk": inputs["L0_attn_gk"],
        "b12": inputs["L0_ffn_b12"], "b3": inputs["L0_ffn_b3"],
    }
    wmap = {k: np.ascontiguousarray(np.asarray(v), dtype=np.float32) for k, v in wmap.items()}
    nc = build_nc(dbg=True)
    in_maps = []
    for c in range(8):
        b, hh = c // 2, c % 2
        own = x[b, hh * T:(hh + 1) * T]
        other = x[b, (1 - hh) * T:(2 - hh) * T]
        xc = np.ascontiguousarray(np.concatenate([own, other], axis=0))
        m = {"x": xc}
        m.update(wmap)
        in_maps.append(m)
    res = run_bass_kernel_spmd(nc, in_maps, core_ids=list(range(8)))
    y = np.empty((4, S, D), dtype=np.float32)
    for c, r in enumerate(res.results):
        b, hh = c // 2, c % 2
        y[b, hh * T:(hh + 1) * T] = np.asarray(r["y"])
    return y


if __name__ == "__main__":
    nc = build_nc()
    print("built ok; instructions:", nc.n_instructions())
